# version 03 (final)
# baseline (speedup 1.0000x reference)
import numpy as np
import ml_dtypes
import concourse.bass as bass
import concourse.mybir as mybir
from concourse.bass_utils import run_bass_kernel_spmd
from concourse.bass_core_rust import IndirectOffsetOnAxis

F32 = mybir.dt.float32
BF16 = mybir.dt.bfloat16
I32 = mybir.dt.int32
AF = mybir.ActivationFunctionType
ALU = mybir.AluOpType
AX = mybir.AxisListType
BF = ml_dtypes.bfloat16


class _Eng:
    def __init__(self, engine, sem, dma_sems=None):
        self.e = engine
        self.sem = sem
        self.n = 0
        self.dma_sems = dma_sems or []
        self.dns = [0] * len(self.dma_sems)
        self.di = 0
        self.seen = {}

    def _wait_all(self, deps):
        best = {}
        for (sem, val) in deps:
            k = sem.name
            if val > best.get(k, (None, -1))[1]:
                best[k] = (sem, val)
        for k, (sem, val) in best.items():
            if sem is self.sem:
                assert val <= self.n, "dep on own future op"
            if self.seen.get(k, -1) >= val:
                continue
            self.e.wait_ge(sem, val)
            self.seen[k] = val

    def op(self, fname, *args, deps=(), inc=True, **kw):
        self._wait_all(list(deps))
        inst = getattr(self.e, fname)(*args, **kw)
        if inc:
            self.n += 1
            inst.then_inc(self.sem, 1)
        return (self.sem, self.n)

    def _slot(self):
        i = self.di % len(self.dma_sems)
        if self.dns[i] > 0:
            self._wait_all([(self.dma_sems[i], self.dns[i])])
        return i

    def _dma_tok(self, i, inst):
        self.di += 1
        self.dns[i] += 16
        inst.then_inc(self.dma_sems[i], 16)
        return (self.dma_sems[i], self.dns[i])

    def dma(self, out, in_, deps=(), **kw):
        i = self._slot()
        self._wait_all(list(deps))
        return self._dma_tok(i, self.e.dma_start(out=out, in_=in_, **kw))

    def dmat(self, out, in_, deps=()):
        i = self._slot()
        self._wait_all(list(deps))
        return self._dma_tok(i, self.e.dma_start_transpose(out, in_))

    def idma(self, out, out_off, in_, in_off, deps=(), **kw):
        i = self._slot()
        self._wait_all(list(deps))
        return self._dma_tok(i, self.e.indirect_dma_start(out, out_off, in_, in_off, **kw))

    def dma_all_tok(self):
        return [(s, n) for s, n in zip(self.dma_sems, self.dns) if n > 0]

    def tok(self):
        return (self.sem, self.n)


class _Em:
    def __init__(self, nc):
        self.nc = nc
        self._ctx = []

    def __enter__(self):
        nc = self.nc

        def mksem(name):
            cm = nc.semaphore(name)
            s = cm.__enter__()
            self._ctx.append(cm)
            return s

        nd = 8
        self.pe = _Eng(nc.tensor, mksem("zs_pe"))
        self.dve = _Eng(nc.vector, mksem("zs_dve"))
        self.act = _Eng(nc.scalar, mksem("zs_act"), [mksem("zd_act%d" % i) for i in range(nd)])
        self.pool = _Eng(nc.gpsimd, mksem("zs_pool"), [mksem("zd_pool%d" % i) for i in range(nd)])
        self.sp = _Eng(nc.sync, mksem("zs_sp"), [mksem("zd_sp%d" % i) for i in range(nd)])
        self.cc_sem = mksem("zd_cc")
        self.cc_n = 0
        return self

    def __exit__(self, *a):
        for cm in reversed(self._ctx):
            cm.__exit__(*a)
        return False

    def collective(self, kind, op, groups, ins, outs, deps=()):
        self.pool._wait_all(list(deps))
        inst = self.nc.gpsimd.collective_compute(kind, op, replica_groups=groups,
                                                 ins=[a.opt() for a in ins],
                                                 outs=[a.opt() for a in outs])
        self.cc_n += 1
        inst.then_inc(self.cc_sem)
        return (self.cc_sem, self.cc_n)

    def phase(self):
        toks = [self.pe.tok(), self.dve.tok(), self.act.tok(), self.pool.tok(), self.sp.tok(),
                (self.cc_sem, self.cc_n)]
        for e in (self.act, self.pool, self.sp):
            toks += e.dma_all_tok()
        return toks


def make_cfg(Sb=8192):
    c = {}
    c["B"], c["Sb"], c["D"], c["nD"] = 2, Sb, 1024, 8
    c["nTb"] = Sb // 128
    c["nT"] = c["nTb"] // 4
    c["T"] = c["nT"] * 128
    c["nJ"] = c["nT"] // 4
    c["Hsw"], c["nH"] = 3072, 24
    c["KVH"] = 4
    c["F"], c["nF"] = 4096, 32
    c["C"] = (c["B"] * Sb) // 8
    c["nC"] = c["C"] // 128
    return c


def _rope_tab(positions, scale, rep):
    inv = 1.0 / 10000.0 ** (np.arange(0, 64, 2, dtype=np.float64) / 64.0)
    fr = np.outer(positions.astype(np.float64), inv)
    co = np.repeat((np.cos(fr) * scale)[:, None, :], rep, axis=1).reshape(len(positions), rep * 32)
    si = np.repeat((np.sin(fr) * scale)[:, None, :], rep, axis=1).reshape(len(positions), rep * 32)
    return co.astype(BF), si.astype(BF)


def host_prep(inputs, cfg):
    c = cfg
    B, Sb, nT, nTb, T, D = c["B"], c["Sb"], c["nT"], c["nTb"], c["T"], c["D"]
    sh = {}
    pc = [{} for _ in range(8)]

    x = np.ascontiguousarray(np.asarray(inputs["L0_attn_x"], dtype=np.float32)).reshape(B, nTb, 128, D)
    for core in range(8):
        b, r = core // 4, core % 4
        pc[core]["xin"] = np.ascontiguousarray(x[b, [4 * j + r for j in range(nT)]].reshape(T, D))

    sh["wgT"] = np.ascontiguousarray(np.asarray(inputs["L0_ffn_w_gate"]).T)
    sh["wuT"] = np.ascontiguousarray(np.asarray(inputs["L0_ffn_w_up"]).T)
    sh["wdT"] = np.ascontiguousarray(np.asarray(inputs["L0_ffn_w_down"]).T)

    sh["wqa"] = np.ascontiguousarray(np.asarray(inputs["L0_attn_Wqa"], np.float32))
    wqb = np.asarray(inputs["L0_attn_Wqb"], np.float32)
    LH = np.zeros(16, np.int64)
    for s in range(16):
        p_ = s // 8
        LH[s] = 8 * p_ + (s % 8) // 2 if s % 2 == 0 else 8 * p_ + 4 + (s % 8 - 1) // 2
    sh["_LH"] = LH
    perm = np.zeros(1024, np.int64)
    for s in range(16):
        for j in range(32):
            perm[s * 64 + j] = LH[s] * 64 + 2 * j
            perm[s * 64 + 32 + j] = LH[s] * 64 + 2 * j + 1
    sh["wqbp"] = np.ascontiguousarray(wqb[:, perm])
    wkvb = np.asarray(inputs["L0_attn_Wkvb"], np.float32)
    kcols = np.zeros(256, np.int64)
    vcols = np.zeros(256, np.int64)
    for h in range(4):
        for j in range(32):
            kcols[h * 64 + j] = h * 128 + 4 * j
            kcols[h * 64 + 32 + j] = h * 128 + 4 * j + 2
        for d in range(64):
            vcols[h * 64 + d] = h * 128 + 2 * d + 1
    sh["wkbp"] = np.ascontiguousarray(wkvb[:, kcols])
    sh["wvb"] = np.ascontiguousarray(wkvb[:, vcols])
    sh["wkva"] = np.ascontiguousarray(np.asarray(inputs["L0_attn_Wkva"], np.float32))
    rperm = np.zeros(1024, np.int64)
    for s in range(16):
        rperm[s * 64:(s + 1) * 64] = np.arange(LH[s] * 64, LH[s] * 64 + 64)
    sh["wout"] = np.ascontiguousarray(np.asarray(inputs["L0_attn_Wout"], np.float32)[rperm, :])
    sh["bout"] = np.ascontiguousarray(np.asarray(inputs["L0_attn_bout"], np.float32).reshape(8, 128).T)

    for core in range(8):
        b, r = core // 4, core % 4
        qpos = np.concatenate([np.arange((4 * j + r) * 128, (4 * j + r) * 128 + 128) for j in range(nT)])
        qc, qs = _rope_tab(qpos, 0.125, 1)
        pc[core]["qcos"] = np.ascontiguousarray(qc.reshape(nT, 128, 32).transpose(1, 0, 2).reshape(128, nT * 32))
        pc[core]["qsin"] = np.ascontiguousarray(qs.reshape(nT, 128, 32).transpose(1, 0, 2).reshape(128, nT * 32))
        kc, ks = _rope_tab(np.arange(Sb), 1.0, 2)
        pc[core]["kcos"] = np.ascontiguousarray(kc.reshape(nTb, 128, 64).transpose(1, 0, 2).reshape(128, nTb * 64))
        pc[core]["ksin"] = np.ascontiguousarray(ks.reshape(nTb, 128, 64).transpose(1, 0, 2).reshape(128, nTb * 64))
        m = np.zeros((128, 8, 2, 4, 128), np.float32)
        low = np.tril(np.ones((128, 128), np.float32))
        for u2 in range(8):
            for vv in range(2):
                kt = 2 * u2 + vv
                for up in range(4):
                    qt = 4 * up + r
                    if kt < qt:
                        m[:, u2, vv, up, :] = 1.0
                    elif kt == qt:
                        m[:, u2, vv, up, :] = low.T
        pc[core]["cmask"] = np.ascontiguousarray(m.reshape(128, 8 * 1024).astype(BF))

    sh["router"] = np.ascontiguousarray(np.asarray(inputs["L1_ffn_router_w"], np.float32))
    w1 = np.asarray(inputs["L1_ffn_w1"], np.float32)
    w2 = np.asarray(inputs["L1_ffn_w2"], np.float32)
    for core in range(8):
        pc[core]["w1e"] = np.ascontiguousarray(w1[core])
        pc[core]["w2e"] = np.ascontiguousarray(w2[core])
    sh["triT"] = np.ascontiguousarray(np.triu(np.ones((128, 128), np.float32), 1).astype(BF))
    nP = nT * 8
    M = np.zeros((nP, nP), np.float32)
    for jp in range(nT):
        for ep in range(8):
            for j in range(nT):
                for e in range(8):
                    if (ep == e and jp < j) or (ep < e):
                        M[jp * 8 + ep, j * 8 + e] = 1.0
    sh["mslot"] = np.ascontiguousarray(M.astype(BF))
    ML = np.zeros((nP, nP), np.float32)
    for jp in range(nT):
        for ep in range(8):
            for j in range(nT):
                if jp < j:
                    ML[jp * 8 + ep, j * 8 + ep] = 1.0
    sh["mlocal"] = np.ascontiguousarray(ML.astype(BF))
    sh["idx8"] = np.ascontiguousarray(np.tile(np.arange(8, dtype=np.float32), (128, 1)))
    for core in range(8):
        b, r = core // 4, core % 4
        A = np.zeros((8 * nT, nT), np.float32)
        for cp in range(8):
            bp, rp = cp // 4, cp % 4
            for jp in range(nT):
                gp = bp * nTb + 4 * jp + rp
                for j in range(nT):
                    if gp < b * nTb + 4 * j + r:
                        A[cp * nT + jp, j] = 1.0
        pc[core]["amat"] = np.ascontiguousarray(A.astype(BF))
        emc = np.zeros((128, 8), np.float32)
        emc2 = np.zeros((128, 8), np.float32)
        emc[:, core] = 1.0
        emc2[:, :core] = 1.0
        pc[core]["emcol"] = np.ascontiguousarray(emc)
        pc[core]["emcol2"] = np.ascontiguousarray(emc2)
        rb = np.zeros((8 * nT, 1), np.float32)
        rb[0:core * nT, 0] = 1.0
        pc[core]["rbcol"] = np.ascontiguousarray(rb.astype(BF))
    TRIG = np.zeros((8 * nT, 8), np.float32)
    RSEL = np.zeros((8 * nT, 8), np.float32)
    for cp in range(8):
        for jp in range(nT):
            for rr in range(8):
                if cp < rr:
                    TRIG[cp * nT + jp, rr] = 1.0
                if cp == rr:
                    RSEL[cp * nT + jp, rr] = 1.0
    sh["trig"] = np.ascontiguousarray(TRIG.astype(BF))
    sh["rsel"] = np.ascontiguousarray(RSEL.astype(BF))

    for nm, wn, bn in (("lwqT", "L1_attn_Wq", "L1_attn_bq"), ("lwkT", "L1_attn_Wk", "L1_attn_bk"),
                       ("lwvT", "L1_attn_Wv", "L1_attn_bv"), ("lwoT", "L1_attn_Wo", "L1_attn_bo")):
        sh[nm] = np.ascontiguousarray(np.asarray(inputs[wn], np.float32).T)
        bb = np.asarray(inputs[bn], np.float32)
        sh[nm + "c"] = np.ascontiguousarray(bb.reshape(8, 128).T)
        sh[nm + "r"] = np.ascontiguousarray(np.tile(bb.astype(BF), (128, 1)))
    return sh, pc


ALL_PARTS = ["sw1", "attn", "sw2", "moe1", "lin", "moe2"]


def build_program(cfg, parts):
    c = cfg
    nT, T, D, nD = c["nT"], c["T"], c["D"], c["nD"]
    nTb, Sb, nJ = c["nTb"], c["Sb"], c["nJ"]
    nH, Hsw = c["nH"], c["Hsw"]
    C, nC, F, nF = c["C"], c["nC"], c["F"], c["nF"]
    nP = nT * 8
    tbT = T // 4
    sbT = min(512, C)
    nSB = C // sbT
    KVH = c["KVH"]

    nc = bass.Bass("TRN2", target_bir_lowering=False)
    DI = lambda n, s, d: nc.dram_tensor(n, s, d, kind="ExternalInput").ap()
    DO = lambda n, s, d: nc.dram_tensor(n, s, d, kind="ExternalOutput").ap()
    DN = lambda n, s, d: nc.dram_tensor(n, s, d, kind="Internal").ap()

    xin = DI("xin", [T, D], F32)
    xout = DO("xout", [T, D], F32)

    need = set(parts)
    g = {}
    if {"sw1", "sw2"} & need:
        g["wgT"] = DI("wgT", [D, Hsw], I32)
        g["wuT"] = DI("wuT", [D, Hsw], I32)
        g["wdT"] = DI("wdT", [Hsw, D], I32)
    if "attn" in need:
        g["wqa"] = DI("wqa", [D, 64], F32)
        g["wqbp"] = DI("wqbp", [64, D], F32)
        g["wkva"] = DI("wkva", [D, 32], F32)
        g["wkbp"] = DI("wkbp", [32, 256], F32)
        g["wvb"] = DI("wvb", [32, 256], F32)
        g["wout"] = DI("wout", [D, D], F32)
        g["bout"] = DI("bout", [128, 8], F32)
        g["qcos"] = DI("qcos", [128, nT * 32], BF16)
        g["qsin"] = DI("qsin", [128, nT * 32], BF16)
        g["kcos"] = DI("kcos", [128, nTb * 64], BF16)
        g["ksin"] = DI("ksin", [128, nTb * 64], BF16)
        g["cmask"] = DI("cmask", [128, 8 * 1024], BF16)
    if {"moe1", "moe2"} & need:
        g["router"] = DI("router", [D, 8], F32)
        g["w1e"] = DI("w1e", [D, F], F32)
        g["w2e"] = DI("w2e", [F, D], F32)
        g["triT"] = DI("triT", [128, 128], BF16)
        g["mslot"] = DI("mslot", [nP, nP], BF16)
        g["mlocal"] = DI("mlocal", [nP, nP], BF16)
        g["rbcol"] = DI("rbcol", [8 * nT, 1], BF16)
        g["idx8"] = DI("idx8", [128, 8], F32)
        g["amat"] = DI("amat", [8 * nT, nT], BF16)
        g["emcol"] = DI("emcol", [128, 8], F32)
        g["emcol2"] = DI("emcol2", [128, 8], F32)
        g["trig"] = DI("trig", [8 * nT, 8], BF16)
        g["rsel"] = DI("rsel", [8 * nT, 8], BF16)
    if "lin" in need:
        for nm in ("lwqT", "lwkT", "lwvT", "lwoT"):
            g[nm] = DI(nm, [D, D], F32)
            g[nm + "c"] = DI(nm + "c", [128, 8], F32)
            g[nm + "r"] = DI(nm + "r", [128, D], BF16)

    t_dram = DN("t_dram", [T, D], BF16)
    o_dram = DN("o_dram", [D, T], BF16)
    if "attn" in need:
        q_dram = DN("q_dram", [T, D], BF16)
        ctx_dram = DN("ctx_dram", [D, T], BF16)
        lat_my = DN("lat_my", [32, T], F32)
        lat_ag = DN("lat_ag", [128, T], F32)
        k_dram = DN("k_dram", [Sb, 128], BF16)
    if {"moe1", "moe2"} & need:
        cnt_my = DN("cnt_my", [1, nP], F32)
        cnt_ag = DN("cnt_ag", [8, nP], F32)
        kcnt_my = DN("kcnt_my", [1, nP], F32)
        kcnt_ag = DN("kcnt_ag", [8, nP], F32)
        pp_dram = DN("pp_dram", [nT, 8], F32)
        r8_dram = DN("r8_dram", [8, 2], F32)
        grp_my = DN("grp_my", [T, D], BF16)
        grp_ag = DN("grp_ag", [8 * T, D], BF16)
        xe_dram = DN("xe_dram", [C, D], BF16)
        yfm_dram = DN("yfm_dram", [D, C], BF16)
        y_my = DN("y_my", [C, D], BF16)
        y_ag = DN("y_ag", [8 * C, D], BF16)
    if "lin" in need:
        lctx_dram = DN("lctx_dram", [T, D], BF16)
        sz_my = DN("sz_my", [128, 520], F32)
        sz_red = DN("sz_red", [128, 520], F32)

    GB = [[0, 1, 2, 3], [4, 5, 6, 7]]
    GA = [[0, 1, 2, 3, 4, 5, 6, 7]]

    from contextlib import ExitStack
    with ExitStack() as _es:
        xb = _es.enter_context(nc.sbuf_tensor("xb", [128, nT * D], F32))
        tfm = _es.enter_context(nc.sbuf_tensor("tfm", [128, nD * T], BF16))
        r2 = _es.enter_context(nc.sbuf_tensor("r2", [128, 16384], BF16))
        r3 = _es.enter_context(nc.sbuf_tensor("r3", [128, 8192], BF16))
        r4 = _es.enter_context(nc.sbuf_tensor("r4", [128, 8192], BF16))
        latb = _es.enter_context(nc.sbuf_tensor("latb", [64, Sb // 2], BF16))
        stg = _es.enter_context(nc.sbuf_tensor("stg", [128, 4096], BF16))
        wsa = _es.enter_context(nc.sbuf_tensor("wsa", [128, 4096], BF16))
        wsb = _es.enter_context(nc.sbuf_tensor("wsb", [128, 6144], BF16))
        prb = _es.enter_context(nc.sbuf_tensor("prb", [128, 2048], BF16))
        sml = _es.enter_context(nc.sbuf_tensor("sml", [128, 512], F32))
        sm2 = _es.enter_context(nc.sbuf_tensor("sm2", [128, 1024], F32))
        on1 = _es.enter_context(nc.sbuf_tensor("on1", [1, 128], BF16))
        onf = _es.enter_context(nc.sbuf_tensor("onf", [1, 128], F32))
        onc = _es.enter_context(nc.sbuf_tensor("onc", [128, 1], BF16))
        idxi = _es.enter_context(nc.sbuf_tensor("idxi", [128, 1], I32))
        pA = _es.enter_context(nc.psum_tensor("pA", [128, 1024], F32))
        pB = _es.enter_context(nc.psum_tensor("pB", [128, 1024], F32))
        pC = _es.enter_context(nc.psum_tensor("pC", [128, 512], F32))
        pD = _es.enter_context(nc.psum_tensor("pD", [128, 512], F32))
        pE = _es.enter_context(nc.psum_tensor("pE", [128, 512], F32))
        pF = _es.enter_context(nc.psum_tensor("pF", [128, 512], F32))
        em = _es.enter_context(_Em(nc))
        blk = _es.enter_context(nc.Block())

        @blk.sync
        def _(_sy):
            pe, dve, act, pool, sp = em.pe, em.dve, em.act, em.pool, em.sp
            NT = (em.sp.sem, 0)
            x_tok = [None] * nT
            ln_last = [NT] * nT

            ld0 = sp.dma(xb.ap().rearrange("p (j d) -> p j d", d=D), xin.rearrange("(j p) d -> p j d", p=128))
            for j in range(nT):
                x_tok[j] = ld0
            c_on1 = dve.op("memset", on1.ap()[:, :], 1.0)
            c_onf = dve.op("memset", onf.ap()[:, :], 1.0)
            c_onc = dve.op("memset", onc.ap()[:, :], 1.0)

            def emit_ln():
                ph = em.phase()
                tile_tok = []
                dma_toks = []
                for j in range(nT):
                    xs = xb.ap()[:, j * D:(j + 1) * D]
                    ss = sml.ap()[:, 0:12]
                    mvs = sml.ap()[:, 12:14]
                    rss = sml.ap()[:, 14:15]
                    nms = sml.ap()[:, 15:16]
                    prev = tile_tok[-1] if tile_tok else NT
                    a = dve.op("bn_stats", ss[:, 0:6], xs[:, 0:512],
                               deps=[x_tok[j], prev] + (ph if j == 0 else []))
                    b = dve.op("bn_stats", ss[:, 6:12], xs[:, 512:1024], deps=[x_tok[j]])
                    cc = dve.op("bn_aggr", mvs, ss, deps=[a, b])
                    d = dve.op("tensor_scalar", rss, mvs[:, 1:2], 1e-5, None, ALU.add, deps=[cc])
                    e1 = act.op("activation", rss, rss, AF.Ln, deps=[d])
                    e2 = act.op("activation", rss, rss, AF.Exp, scale=-0.5, deps=[e1])
                    f = dve.op("tensor_scalar", nms, mvs[:, 0:1], -1.0, None, ALU.mult, deps=[e2])
                    ts = stg.ap()[:, (j % 2) * D:(j % 2) * D + D]
                    gg = dve.op("tensor_scalar", ts, xs, nms[:, 0:1], rss[:, 0:1], ALU.add, ALU.mult,
                                deps=[f] + ([dma_toks[j - 2]] if j >= 2 else []))
                    tile_tok.append(gg)
                    ln_last[j] = gg
                    dma_toks.append(sp.dma(t_dram[j * 128:(j + 1) * 128, :], ts, deps=[gg]))
                tf_toks = []
                for ch in range(nD):
                    tf_toks.append(act.dmat(tfm.ap()[:, ch * T:(ch + 1) * T],
                                            t_dram[:, ch * 128:(ch + 1) * 128], deps=dma_toks + ph))
                return tf_toks, dma_toks

            def emit_resid(dep_toks):
                toks = []
                for j in range(nT):
                    ss = stg.ap()[:, 2048 + (j % 2) * D:2048 + (j % 2) * D + D]
                    lt = act.dmat(ss, o_dram[:, j * 128:(j + 1) * 128],
                                  deps=list(dep_toks) + ([toks[j - 2]] if j >= 2 else []))
                    xs = xb.ap()[:, j * D:(j + 1) * D]
                    t = dve.op("tensor_tensor", xs, xs, ss, ALU.add, deps=[lt, x_tok[j], ln_last[j]])
                    x_tok[j] = t
                    toks.append(t)
                return toks

            def emit_swiglu():
                tf_toks, _ = emit_ln()
                out_dma = []
                wpar_tok = [NT, NT]
                h_cons = [NT] * nH
                for tb in range(4):
                    hv_toks = []
                    for hc in range(nH):
                        par = hc % 2
                        wga = wsa.ap()[:, par * 2048:par * 2048 + 1024]
                        wua = wsa.ap()[:, par * 2048 + 1024:par * 2048 + 2048]
                        lw1 = pool.dma(wga.rearrange("p (dc h) -> p dc h", h=128), g["wgT"][:, hc * 128:(hc + 1) * 128].rearrange("(dc p) h -> p dc h", p=128), deps=[wpar_tok[par]])
                        lw2 = pool.dma(wua.rearrange("p (dc h) -> p dc h", h=128), g["wuT"][:, hc * 128:(hc + 1) * 128].rearrange("(dc p) h -> p dc h", p=128))
                        pg = pA.ap()[:, par * 512:par * 512 + tbT]
                        pu = pB.ap()[:, par * 512:par * 512 + tbT]
                        dep0 = tf_toks + [lw1, lw2] + ([hv_toks[hc - 2]] if hc >= 2 else [])
                        for dc in range(nD):
                            pe.op("matmul", pg, wga[:, dc * 128:(dc + 1) * 128],
                                  tfm.ap()[:, dc * T + tb * tbT:dc * T + (tb + 1) * tbT],
                                  start=(dc == 0), stop=(dc == nD - 1),
                                  deps=dep0 if dc == 0 else (), inc=False)
                            pe.op("matmul", pu, wua[:, dc * 128:(dc + 1) * 128],
                                  tfm.ap()[:, dc * T + tb * tbT:dc * T + (tb + 1) * tbT],
                                  start=(dc == 0), stop=(dc == nD - 1), inc=(dc == nD - 1))
                        mm = pe.tok()
                        wpar_tok[par] = mm
                        gb = prb.ap()[:, par * 1024:par * 1024 + tbT]
                        gs = act.op("activation", gb, pg, AF.Silu, scale=1.0 / 127.0, deps=[mm])
                        hstr = r2.ap()[:, hc * tbT:(hc + 1) * tbT]
                        hv = dve.op("scalar_tensor_tensor", hstr, pu, 1.0 / 127.0, gb,
                                    ALU.mult, ALU.mult, deps=[gs, h_cons[hc]])
                        hv_toks.append(hv)
                    wd_tok = [NT, NT]
                    ev_toks = []
                    for dc in range(nD):
                        par = dc % 2
                        wda = wsb.ap()[:, par * 3072:par * 3072 + 3072]
                        lw = pool.dma(wda.rearrange("p (hc d) -> p hc d", d=128), g["wdT"][:, dc * 128:(dc + 1) * 128].rearrange("(hc p) d -> p hc d", p=128), deps=[wd_tok[par]])
                        po = pC.ap() if par == 0 else pD.ap()
                        for hc in range(nH):
                            pe.op("matmul", po[:, 0:tbT], wda[:, hc * 128:(hc + 1) * 128],
                                  r2.ap()[:, hc * tbT:(hc + 1) * tbT],
                                  start=(hc == 0), stop=(hc == nH - 1),
                                  deps=([lw] + hv_toks + ([ev_toks[dc - 2]] if dc >= 2 else [])) if hc == 0 else (),
                                  inc=(hc == nH - 1))
                        mm = pe.tok()
                        wd_tok[par] = mm
                        ob = stg.ap()[:, 2048 + par * 1024:2048 + par * 1024 + tbT]
                        ev = act.op("activation", ob, po[:, 0:tbT], AF.Copy, scale=0.5 / 127.0,
                                    deps=[mm] + ([out_dma[-2]] if len(out_dma) >= 2 else []))
                        ev_toks.append(ev)
                        out_dma.append(sp.dma(o_dram[dc * 128:(dc + 1) * 128, tb * tbT:(tb + 1) * tbT],
                                              ob, deps=[ev]))
                    h_cons = [pe.tok()] * nH
                emit_resid(out_dma)

            def emit_attn():
                tf_toks, _ = emit_ln()
                nHB = nTb // 2

                def lat_slice(jt):
                    hb = jt // nHB
                    return latb.ap()[hb * 32:hb * 32 + 32,
                                     (jt - hb * nHB) * 128:(jt - hb * nHB) * 128 + 128]
                qlb = stg.ap()[0:64, 2048:2048 + T]
                lwkva = pool.dma(wsa.ap()[:, 0:256].rearrange("p (dc l) -> p dc l", l=32), g["wkva"].rearrange("(dc p) l -> p dc l", p=128), deps=em.phase())
                lwqa = pool.dma(wsa.ap()[:, 256:768].rearrange("p (dc l) -> p dc l", l=64), g["wqa"].rearrange("(dc p) l -> p dc l", p=128))
                lwqb = pool.dma(wsa.ap()[0:64, 768:768 + 1024], g["wqbp"])
                lwkb = pool.dma(wsa.ap()[0:32, 1792:1792 + 256], g["wkbp"])
                lwkb2 = pool.dma(wsa.ap()[32:64, 1792:1792 + 256], g["wkbp"])
                lwvb = pool.dma(wsa.ap()[0:32, 2048:2048 + 256], g["wvb"])
                lwvb2 = pool.dma(wsa.ap()[32:64, 2048:2048 + 256], g["wvb"])
                lmask = pool.dma(r4.ap()[:, 0:8192], g["cmask"])
                lat_evs = []
                nW = max(T // 512, 1)
                W = T // nW
                for gq in range(nW):
                    pl = pC.ap()[0:32, 0:W]
                    for dc in range(nD):
                        pe.op("matmul", pl, wsa.ap()[:, dc * 32:(dc + 1) * 32],
                              tfm.ap()[:, dc * T + gq * W:dc * T + (gq + 1) * W],
                              start=(dc == 0), stop=(dc == nD - 1),
                              deps=(tf_toks + [lwkva] + lat_evs[-1:]) if dc == 0 else (), inc=(dc == nD - 1))
                    mm = pe.tok()
                    ev = dve.op("tensor_copy", sml.ap()[0:32, 0:W], pl, deps=[mm, lat_evs[-1] if lat_evs else NT])
                    lat_evs.append(sp.dma(lat_my[:, gq * W:(gq + 1) * W], sml.ap()[0:32, 0:W], deps=[ev]))
                ag = em.collective("AllGather", ALU.bypass, GB, [lat_my], [lat_ag], deps=lat_evs)
                lat_rds = []
                nJH = nHB // 4
                for hb in range(2):
                    for rr in range(4):
                        dst = latb.ap()[hb * 32:hb * 32 + 32, :] \
                            .rearrange("l (j rr t) -> l j rr t", rr=4, t=128)[:, :, rr, :]
                        srcv = lat_ag.rearrange("(rr l) (j t) -> l j rr t", rr=4, t=128)[:, hb * nJH:(hb + 1) * nJH, rr, :]
                        lat_rds.append(pool.dma(dst, srcv, deps=[ag]))
                lat_rd = lat_rds[-1]
                ql_evs = []
                for gq in range(nW):
                    pl = pD.ap()[0:64, 0:W]
                    for dc in range(nD):
                        pe.op("matmul", pl, wsa.ap()[:, 256 + dc * 64:256 + (dc + 1) * 64],
                              tfm.ap()[:, dc * T + gq * W:dc * T + (gq + 1) * W],
                              start=(dc == 0), stop=(dc == nD - 1),
                              deps=(tf_toks + [lwqa] + ql_evs[-1:]) if dc == 0 else (), inc=(dc == nD - 1))
                    mm = pe.tok()
                    ql_evs.append(dve.op("tensor_copy", qlb[:, gq * W:(gq + 1) * W], pl, deps=[mm]))
                qdma = []
                rprev = (em.dve.sem, em.dve.n)
                for j in range(nT):
                    pq = pA.ap()
                    pe.op("matmul", pq[:, 0:512], qlb[:, j * 128:(j + 1) * 128], wsa.ap()[0:64, 768:768 + 512],
                          start=True, stop=True,
                          deps=ql_evs + [lwqb] + ([qdma[j - 2]] if j >= 2 else []) + [rprev], inc=False)
                    pe.op("matmul", pq[:, 512:1024], qlb[:, j * 128:(j + 1) * 128],
                          wsa.ap()[0:64, 768 + 512:768 + 1024], start=True, stop=True)
                    mmt = pe.tok()
                    tc = prb.ap()[:, 0:32]
                    tsn = prb.ap()[:, 32:64]
                    l1 = sp.dma(tc, g["qcos"][:, j * 32:(j + 1) * 32], deps=[rprev])
                    l2 = sp.dma(tsn, g["qsin"][:, j * 32:(j + 1) * 32])
                    st_ = stg.ap()[:, (j % 2) * D:(j % 2) * D + D]
                    pqr = pq.rearrange("p (h s j) -> p h s j", s=2, j=32)
                    str_ = st_.rearrange("p (h s j) -> p h s j", s=2, j=32)
                    x1, x2 = pqr[:, :, 0, :], pqr[:, :, 1, :]
                    r1, r2v = str_[:, :, 0, :], str_[:, :, 1, :]
                    tcb = tc.unsqueeze(1).broadcast_to((128, 16, 32))
                    tsb = tsn.unsqueeze(1).broadcast_to((128, 16, 32))
                    a1 = sm2.ap()[:, 0:512].rearrange("p (h j) -> p h j", j=32)
                    a2 = sm2.ap()[:, 512:1024].rearrange("p (h j) -> p h j", j=32)
                    o1 = dve.op("tensor_tensor", a1, x1, tcb, ALU.mult, deps=[mmt, l1, l2, rprev])
                    o2 = dve.op("tensor_tensor", a2, x2, tsb, ALU.mult)
                    o3 = dve.op("tensor_tensor", r1, a1, a2, ALU.subtract, deps=[o1, o2])
                    o4 = dve.op("tensor_tensor", a1, x2, tcb, ALU.mult, deps=[o3])
                    o5 = dve.op("tensor_tensor", a2, x1, tsb, ALU.mult)
                    o6 = dve.op("tensor_tensor", r2v, a1, a2, ALU.add, deps=[o4, o5])
                    rprev = o6
                    qdma.append(sp.dma(q_dram[j * 128:(j + 1) * 128, :], st_, deps=[o6]))
                qfm = tfm
                qf_toks = []
                for ch in range(nD):
                    qf_toks.append(act.dmat(qfm.ap()[:, ch * T:(ch + 1) * T],
                                            q_dram[:, ch * 128:(ch + 1) * 128],
                                            deps=qdma + [pe.tok()]))
                vtm = r2.ap()[:, 0:2 * nTb * 65]
                ctxs = r2.ap()[:, 2 * nTb * 65:2 * nTb * 65 + 4096]
                kfm = r3.ap()[:, 0:Sb]
                flash_tok = [NT]
                epil_toks = [NT, NT]
                ep_prev = NT
                bc_prev = NT
                for pair in range(KVH // 2):
                    kt_toks = []
                    rp = (em.dve.sem, em.dve.n)
                    for jb in range(0, nTb, 4):
                        pk = pB.ap()
                        for jj in range(4):
                            jt = jb + jj
                            rb = 32 * (jt // nHB)
                            pe.op("matmul", pk[:, jj * 128:jj * 128 + 128],
                                  lat_slice(jt),
                                  wsa.ap()[rb:rb + 32, 1792 + pair * 128:1792 + (pair + 1) * 128],
                                  start=True, stop=True,
                                  deps=([lwkb, lwkb2, rp, flash_tok[-1]] + qf_toks + lat_rds) if jj == 0 else (),
                                  inc=(jj == 3))
                        mmk = pe.tok()
                        tck = prb.ap()[:, 0:256]
                        tsk = prb.ap()[:, 256:512]
                        lk1 = sp.dma(tck, g["kcos"][:, jb * 64:(jb + 4) * 64], deps=[rp, flash_tok[-1]])
                        lk2 = sp.dma(tsk, g["ksin"][:, jb * 64:(jb + 4) * 64])
                        st2 = stg.ap()[:, 0:512]
                        pkr = pk[:, 0:512].rearrange("p (t h s j) -> p t h s j", h=2, s=2, j=32)
                        s2r = st2.rearrange("p (t h s j) -> p t h s j", h=2, s=2, j=32)
                        x1, x2 = pkr[:, :, :, 0, :], pkr[:, :, :, 1, :]
                        r1, r2v = s2r[:, :, :, 0, :], s2r[:, :, :, 1, :]
                        tcb = tck.rearrange("p (h j) -> p h j", j=32).unsqueeze(1).broadcast_to((128, 4, 2, 32)) \
                            if False else tck.rearrange("p (t hj) -> p t hj", t=4).rearrange("p t (h j) -> p t h j", j=32)
                        tsb = tsk.rearrange("p (t hj) -> p t hj", t=4).rearrange("p t (h j) -> p t h j", j=32)
                        a1 = sm2.ap()[:, 0:256].rearrange("p (t h j) -> p t h j", h=2, j=32)
                        a2 = sm2.ap()[:, 256:512].rearrange("p (t h j) -> p t h j", h=2, j=32)
                        o1 = dve.op("tensor_tensor", a1, x1, tcb, ALU.mult,
                                    deps=[mmk, lk1, lk2, rp] + kt_toks[-1:])
                        o2 = dve.op("tensor_tensor", a2, x2, tsb, ALU.mult)
                        o3 = dve.op("tensor_tensor", r1, a1, a2, ALU.subtract, deps=[o1, o2])
                        o4 = dve.op("tensor_tensor", a1, x2, tcb, ALU.mult, deps=[o3])
                        o5 = dve.op("tensor_tensor", a2, x1, tsb, ALU.mult)
                        o6 = dve.op("tensor_tensor", r2v, a1, a2, ALU.add, deps=[o4, o5])
                        rp = o6
                        kt_toks.append(sp.dma(k_dram[jb * 128:(jb + 4) * 128, :].rearrange("(t p) d -> p t d", p=128), st2.rearrange("p (t d) -> p t d", d=128), deps=[o6]))
                    kread = act.dmat(kfm, k_dram, deps=kt_toks + [flash_tok[-1]])
                    v_toks = []
                    for jb in range(0, nTb, 4):
                        pv = pB.ap()
                        for jj in range(4):
                            jt = jb + jj
                            rb = 32 * (jt // nHB)
                            pe.op("matmul", pv[:, jj * 128:jj * 128 + 128],
                                  lat_slice(jt),
                                  wsa.ap()[rb:rb + 32, 2048 + pair * 128:2048 + (pair + 1) * 128],
                                  start=True, stop=True,
                                  deps=[lwvb, lwvb2, kread, v_toks[-1] if v_toks else flash_tok[-1]] if jj == 0 else (),
                                  inc=(jj == 3))
                        mmv = pe.tok()
                        for jj in range(4):
                            jt = jb + jj
                            for hh in range(2):
                                v_toks.append(act.op(
                                    "activation", vtm[:, hh * nTb * 65 + jt * 65:hh * nTb * 65 + jt * 65 + 64],
                                    pv[:, jj * 128 + hh * 64:jj * 128 + hh * 64 + 64], AF.Copy, deps=[mmv]))
                    vones = dve.op("memset", vtm.rearrange("p (z o) -> p z o", o=65)[:, :, 64:65], 1.0,
                                   deps=[flash_tok[-1]])
                    pair_av = []
                    for hh in range(8):
                        h = pair * 8 + hh
                        kvl = hh % 2
                        qrow = (h % 2) * 64
                        qch = h // 2
                        for J in range(nJ):
                            nhw = 8 * J + 8
                            cix = (hh * nJ + J) % 2
                            pctx = (pC if cix == 0 else pD).ap()[0:65, :]
                            exp_toks = [NT, NT]
                            avd = [NT, NT]
                            for hw in range(nhw):
                                par = hw % 2
                                psc = (pA if par == 0 else pB).ap()
                                mm0 = None
                                for v in range(2):
                                    kt = 2 * hw + v
                                    pe.op("matmul", psc[:, v * 512:v * 512 + 512],
                                          kfm[kvl * 64:kvl * 64 + 64, kt * 128:(kt + 1) * 128],
                                          qfm.ap()[qrow:qrow + 64, qch * T + J * 512:qch * T + (J + 1) * 512],
                                          start=True, stop=True,
                                          deps=([kread, exp_toks[par]] + qf_toks + [vones]) if v == 0 else (),
                                          inc=(v == 1))
                                mm = pe.tok()
                                pb_ = prb.ap()[:, par * 1024:par * 1024 + 1024]
                                ex = act.op("activation", pb_, psc, AF.Exp, deps=[mm, avd[par]])
                                exp_toks[par] = ex
                                u2 = hw - 8 * J
                                if u2 >= 0:
                                    ex = dve.op("tensor_tensor", pb_, pb_,
                                                r4.ap()[:, u2 * 1024:(u2 + 1) * 1024], ALU.mult,
                                                deps=[ex, lmask])
                                    exp_toks[par] = ex
                                for v in range(2):
                                    kt = 2 * hw + v
                                    pe.op("matmul", pctx,
                                          vtm[:, kvl * nTb * 65 + kt * 65:kvl * nTb * 65 + kt * 65 + 65],
                                          pb_[:, v * 512:v * 512 + 512],
                                          start=(hw == 0 and v == 0), stop=(hw == nhw - 1 and v == 1),
                                          deps=([ex] + ([epil_toks[cix]] if hw == 0 else [])) if v == 0 else (),
                                          inc=(v == 1))
                                avd[par] = pe.tok()
                            avl = pe.tok()
                            pair_av.append(avl)
                            rrb = prb.ap()[:, 0:0]
                            rr0 = dve.op("reciprocal", sml.ap()[0:1, 0:512], pctx[64:65, :], deps=[avl])
                            rr = dve.op("tensor_copy", stg.ap()[0:1, 1024:1024 + 512], sml.ap()[0:1, 0:512],
                                        deps=[rr0, bc_prev])
                            bc = pe.op("matmul", pE.ap()[0:64, :], on1.ap()[0:1, 0:64],
                                       stg.ap()[0:1, 1024:1024 + 512], start=True, stop=True, deps=[rr])
                            bc_prev = bc
                            cm = act.op("activation", stg.ap()[0:64, 1536:1536 + 512],
                                        pctx[0:64, :], AF.Copy, deps=[avl, ep_prev])
                            ct = dve.op("tensor_tensor", stg.ap()[0:64, 1024 + 1024:1024 + 1024 + 0] if False else
                                        stg.ap()[0:64, 1536:1536 + 512],
                                        stg.ap()[0:64, 1536:1536 + 512], pE.ap()[0:64, :], ALU.mult,
                                        deps=[cm, bc])
                            ep = sp.dma(ctx_dram[qch * 128 + qrow:qch * 128 + qrow + 64, J * 512:(J + 1) * 512],
                                        stg.ap()[0:64, 1536:1536 + 512], deps=[ct])
                            epil_toks[cix] = ep
                            ep_prev = ep
                    flash_tok.append(pe.tok())
                lbout = sp.dma(sml.ap()[:, 16:24], g["bout"], deps=[])
                oev_prev = [NT, NT]
                odma = []
                for tb in range(4):
                    cld = []
                    for ec in range(nD):
                        cld.append(sp.dma(ctxs[:, ec * 512:ec * 512 + tbT],
                                          ctx_dram[ec * 128:(ec + 1) * 128, tb * tbT:(tb + 1) * tbT],
                                          deps=[flash_tok[-1]] + epil_toks))
                    for oc in range(nD):
                        par = oc % 2
                        wst = wsb.ap()[:, par * 1024:par * 1024 + 1024]
                        lw = pool.dma(wst.rearrange("p (ec o) -> p ec o", o=128), g["wout"][:, oc * 128:(oc + 1) * 128].rearrange("(ec p) o -> p ec o", p=128), deps=[oev_prev[par]])
                        po = (pC if par == 0 else pD).ap()
                        for ec in range(nD):
                            pe.op("matmul", po[:, 0:tbT], wst[:, ec * 128:(ec + 1) * 128],
                                  ctxs[:, ec * 512:ec * 512 + tbT],
                                  start=(ec == 0), stop=(ec == nD - 1),
                                  deps=(cld + [lw, oev_prev[par]]) if ec == 0 else (), inc=(ec == nD - 1))
                        mm = pe.tok()
                        ob = stg.ap()[:, (par) * 1024:(par) * 1024 + tbT]
                        ev = dve.op("tensor_scalar", ob, po[:, 0:tbT], sml.ap()[:, 16 + oc:17 + oc], None,
                                    ALU.add, deps=[mm, lbout, odma[-2] if len(odma) >= 2 else NT])
                        oev_prev[par] = ev
                        odma.append(sp.dma(o_dram[oc * 128:(oc + 1) * 128, tb * tbT:(tb + 1) * tbT], ob,
                                           deps=[ev]))
                emit_resid(odma)

            def emit_moe():
                tf_toks, tdma = emit_ln()
                ph = em.phase()
                lrt = pool.dma(wsa.ap()[:, 0:64].rearrange("p (dc e) -> p dc e", e=8), g["router"].rearrange("(dc p) e -> p dc e", p=128), deps=ph)
                ltri = pool.dma(wsa.ap()[:, 64:192], g["triT"], deps=ph)
                lmsl = pool.dma(wsa.ap()[0:nP, 192:192 + nP], g["mslot"], deps=ph)
                lam = pool.dma(wsa.ap()[0:8 * nT, 512:512 + nT], g["amat"], deps=ph)
                ltrg = pool.dma(wsa.ap()[0:8 * nT, 640:648], g["trig"], deps=ph)
                lrsl = pool.dma(wsa.ap()[0:8 * nT, 648:656], g["rsel"], deps=ph)
                li8 = sp.dma(sml.ap()[:, 0:8], g["idx8"], deps=ph)
                lec = sp.dma(sm2.ap()[:, 0:8], g["emcol"], deps=ph)
                lec2 = sp.dma(sm2.ap()[:, 8:16], g["emcol2"], deps=ph)
                gk = sm2.ap()[:, 16:16 + nT]
                eti = sm2.ap()[:, 32:32 + nT]
                yrow = sm2.ap()[:, 48:48 + nT]
                cum_all = sm2.ap()[:, 64:64 + nP]
                onehot_all = r2.ap()[:, 0:nP]
                cnt_evs = []
                for j in range(nT):
                    pl = pC.ap()[:, 0:8]
                    for dc in range(nD):
                        pe.op("matmul", pl, tfm.ap()[:, dc * T + j * 128:dc * T + (j + 1) * 128],
                              wsa.ap()[:, dc * 8:dc * 8 + 8],
                              start=(dc == 0), stop=(dc == nD - 1),
                              deps=(tf_toks + [lrt, cnt_evs[-1] if cnt_evs else NT]) if dc == 0 else (),
                              inc=(dc == nD - 1))
                    mm = pe.tok()
                    lg = sml.ap()[:, 16:24]
                    mx = sml.ap()[:, 24:32]
                    nmx = sml.ap()[:, 32:33]
                    smx = sml.ap()[:, 33:34]
                    t0 = dve.op("tensor_copy", lg, pl, deps=[mm, cnt_evs[-1] if cnt_evs else NT])
                    t1 = dve.op("max", mx, lg, deps=[t0])
                    oh = onehot_all[:, j * 8:(j + 1) * 8]
                    t3 = dve.op("tensor_scalar", oh, lg, mx[:, 0:1], None, ALU.is_equal, deps=[t1])
                    t4 = dve.op("tensor_reduce", nmx, lg, AX.X, ALU.max, negate=True, deps=[t1])
                    t5 = act.op("activation", sml.ap()[:, 42:50], lg, AF.Exp, bias=nmx[:, 0:1],
                                accum_out=smx[:, 0:1], deps=[t4])
                    t6 = dve.op("reciprocal", sml.ap()[:, 34:35], smx[:, 0:1], deps=[t5])
                    t7 = dve.op("tensor_scalar", gk[:, j:j + 1], sml.ap()[:, 34:35], 0.5, None, ALU.mult,
                                deps=[t6])
                    t8 = dve.op("tensor_tensor", sml.ap()[:, 42:50], oh, sml.ap()[:, 0:8], ALU.mult,
                                deps=[t3, li8, t5])
                    t9 = dve.op("reduce_sum", eti[:, j:j + 1], sml.ap()[:, 42:50], AX.X, deps=[t8])
                    pcm = pE.ap()[:, 0:8]
                    mmc = pe.op("matmul", pcm, wsa.ap()[:, 64:192], oh, start=True, stop=True,
                                deps=[t3, ltri, cnt_evs[-1] if cnt_evs else NT])
                    mmt = pe.op("matmul", pF.ap()[0:1, j * 8:(j + 1) * 8], onc.ap()[:, 0:1], oh,
                                start=True, stop=True, deps=[c_onc])
                    cv = dve.op("tensor_copy", cum_all[:, j * 8:(j + 1) * 8], pcm, deps=[mmc, mmt])
                    cnt_evs.append(cv)
                cntc = dve.op("tensor_copy", sml.ap()[0:1, 50:50 + nP], pF.ap()[0:1, 0:nP], deps=[cnt_evs[-1]])
                cdma = sp.dma(cnt_my, sml.ap()[0:1, 50:50 + nP], deps=[cntc])
                agc = em.collective("AllGather", ALU.bypass, GA, [cnt_my], [cnt_ag], deps=[cdma])
                nQ = 8 * nT
                cbf = pool.dma(r2.ap()[0:nQ, nP:nP + 8], cnt_ag.rearrange("cc (j e) -> (cc j) e", e=8),
                               deps=[agc])
                pmm = pe.op("matmul", pE.ap()[0:nT, 8:16], wsa.ap()[0:nQ, 512:512 + nT],
                            r2.ap()[0:nQ, nP:nP + 8], start=True, stop=True, deps=[cbf, lam])
                pev = dve.op("tensor_copy", sm2.ap()[0:nT, 864:872], pE.ap()[0:nT, 8:16], deps=[pmm, cntc])
                ppd = sp.dma(pp_dram, sm2.ap()[0:nT, 864:872], deps=[pev])
                prow = sp.dma(sml.ap()[0:1, 320:320 + nP], pp_dram.rearrange("j e -> (j e)").unsqueeze(0),
                              deps=[ppd])
                kcum_all = sm2.ap()[:, 192:192 + nP]
                ohk_all = r2.ap()[:, 2 * nP:2 * nP + nP]
                tok_prev = cnt_evs[-1]
                kc_evs = []
                for j in range(nT):
                    oh = onehot_all[:, j * 8:(j + 1) * 8]
                    b2 = pe.op("matmul", pE.ap()[:, 40:48], onf.ap()[0:1, :],
                               sml.ap()[0:1, 320 + j * 8:320 + (j + 1) * 8],
                               start=True, stop=True, deps=[prow, c_onf, tok_prev])
                    g1 = dve.op("tensor_tensor", sml.ap()[:, 42:50], pE.ap()[:, 40:48], oh, ALU.mult,
                                deps=[b2, tok_prev])
                    g2 = dve.op("reduce_sum", sml.ap()[:, 38:39], sml.ap()[:, 42:50], AX.X, deps=[g1])
                    s2 = dve.op("tensor_tensor", sml.ap()[:, 42:50], cum_all[:, j * 8:(j + 1) * 8], oh,
                                ALU.mult, deps=[g2])
                    s4 = dve.op("reduce_sum", sml.ap()[:, 36:37], sml.ap()[:, 42:50], AX.X, deps=[s2])
                    pos = dve.op("tensor_tensor", sml.ap()[:, 39:40], sml.ap()[:, 38:39], sml.ap()[:, 36:37],
                                 ALU.add, deps=[g2, s4])
                    kp = dve.op("tensor_scalar", sml.ap()[:, 40:41], sml.ap()[:, 39:40], float(C), None,
                                ALU.is_lt, deps=[pos])
                    gku = dve.op("tensor_tensor", gk[:, j:j + 1], gk[:, j:j + 1], sml.ap()[:, 40:41],
                                 ALU.mult, deps=[kp])
                    ohk = ohk_all[:, j * 8:(j + 1) * 8]
                    okk = dve.op("tensor_scalar", ohk, oh, sml.ap()[:, 40:41], None, ALU.mult, deps=[gku])
                    pcm = pE.ap()[:, 0:8]
                    mmc = pe.op("matmul", pcm, wsa.ap()[:, 64:192], ohk, start=True, stop=True,
                                deps=[okk, ltri] + kc_evs[-1:])
                    mmt = pe.op("matmul", pF.ap()[0:1, 256 + j * 8:256 + (j + 1) * 8], onc.ap()[:, 0:1], ohk,
                                start=True, stop=True, deps=[c_onc])
                    cv = dve.op("tensor_copy", kcum_all[:, j * 8:(j + 1) * 8], pcm, deps=[mmc, mmt])
                    kc_evs.append(cv)
                    tok_prev = cv
                kcntc = dve.op("tensor_copy", sml.ap()[0:1, 50:50 + nP], pF.ap()[0:1, 256:256 + nP],
                               deps=[kc_evs[-1], cdma])
                kdma = sp.dma(kcnt_my, sml.ap()[0:1, 50:50 + nP], deps=[kcntc])
                agk = em.collective("AllGather", ALU.bypass, GA, [kcnt_my], [kcnt_ag], deps=[kdma])
                ccol = pool.dma(prb.ap()[0:nP, 0:1], kcnt_my.rearrange("o p -> p o"), deps=[kdma],
                                allow_slow_non_contiguous=True)
                kb_mm = pe.op("matmul", pE.ap()[0:1, 256:256 + nP], prb.ap()[0:nP, 0:1],
                              wsa.ap()[0:nP, 192:192 + nP], start=True, stop=True, deps=[ccol, lmsl])
                kbrow = dve.op("tensor_copy", sml.ap()[0:1, 192:192 + nP], pE.ap()[0:1, 256:256 + nP],
                               deps=[kb_mm, kcntc])
                lml = pool.dma(wsa.ap()[0:nP, 768:768 + nP], g["mlocal"], deps=ph)
                kl_mm = pe.op("matmul", pE.ap()[0:1, 256:256 + nP], prb.ap()[0:nP, 0:1],
                              wsa.ap()[0:nP, 768:768 + nP], start=True, stop=True, deps=[ccol, lml, kbrow])
                klrow = dve.op("tensor_copy", sm2.ap()[0:1, 896:896 + nP], pE.ap()[0:1, 256:256 + nP],
                               deps=[kl_mm])
                lrb = pool.dma(wsa.ap()[0:nQ, 896:897], g["rbcol"], deps=ph)
                kbf = pool.dma(r2.ap()[0:nQ, 2 * nP + nP:2 * nP + nP + 8],
                               kcnt_ag.rearrange("cc (j e) -> (cc j) e", e=8), deps=[agk])
                rb_mm = pe.op("matmul", pE.ap()[0:1, 248:256], wsa.ap()[0:nQ, 896:897],
                              r2.ap()[0:nQ, 2 * nP + nP:2 * nP + nP + 8], start=True, stop=True,
                              deps=[lrb, kbf, klrow])
                rbrow = dve.op("tensor_copy", sml.ap()[0:1, 466:474], pE.ap()[0:1, 248:256], deps=[rb_mm])
                zf = dve.op("memset", stg.ap()[:, 3072:4096], 0.0, deps=[tok_prev])
                zf_toks = []
                for j in range(nT):
                    zf_toks.append(sp.dma(grp_my[j * 128:(j + 1) * 128, :], stg.ap()[:, 3072:4096],
                                          deps=[zf]))
                sc_toks = []
                for j in range(nT):
                    ohk = ohk_all[:, j * 8:(j + 1) * 8]
                    b1 = pe.op("matmul", pE.ap()[:, 32:40], onf.ap()[0:1, :],
                               sml.ap()[0:1, 192 + j * 8:192 + (j + 1) * 8],
                               start=True, stop=True, deps=[kbrow, c_onf, tok_prev])
                    b3 = pe.op("matmul", pE.ap()[:, 48:56], onf.ap()[0:1, :],
                               sm2.ap()[0:1, 896 + j * 8:896 + (j + 1) * 8],
                               start=True, stop=True, deps=[klrow])
                    b4 = pe.op("matmul", pE.ap()[:, 56:64], onf.ap()[0:1, :], sml.ap()[0:1, 466:474],
                               start=True, stop=True, deps=[rbrow])
                    s1 = dve.op("tensor_tensor", sml.ap()[:, 42:50], pE.ap()[:, 32:40], ohk, ALU.mult,
                                deps=[b1, b3, b4, tok_prev])
                    s3 = dve.op("reduce_sum", sml.ap()[:, 35:36], sml.ap()[:, 42:50], AX.X, deps=[s1])
                    s2 = dve.op("tensor_tensor", sml.ap()[:, 42:50], kcum_all[:, j * 8:(j + 1) * 8], ohk,
                                ALU.mult, deps=[s3])
                    s4 = dve.op("reduce_sum", sml.ap()[:, 36:37], sml.ap()[:, 42:50], AX.X, deps=[s2])
                    slot = dve.op("tensor_tensor", sml.ap()[:, 37:38], sml.ap()[:, 35:36], sml.ap()[:, 36:37],
                                  ALU.add, deps=[s3, s4])
                    nk = dve.op("tensor_scalar", sml.ap()[:, 34:35], sml.ap()[:, 40 - 40 + 40:41] if False else sml.ap()[:, 40:41],
                                -float(T), float(T), ALU.mult, ALU.add, deps=[slot])
                    sl2 = dve.op("tensor_tensor", sml.ap()[:, 37:38], sml.ap()[:, 37:38], sml.ap()[:, 34:35],
                                 ALU.add, deps=[nk])
                    ii = dve.op("tensor_copy", idxi.ap()[:, 0:1], sml.ap()[:, 37:38], deps=[sl2])
                    tl = sp.dma(stg.ap()[:, (j % 2) * D:(j % 2) * D + D], t_dram[j * 128:(j + 1) * 128, :],
                                deps=tdma + sc_toks[-2:-1])
                    sc = pool.idma(grp_my, IndirectOffsetOnAxis(ap=idxi.ap()[:, 0:1], axis=0),
                                   stg.ap()[:, (j % 2) * D:(j % 2) * D + D], None,
                                   bounds_check=T - 1, oob_is_err=False, deps=[ii, tl] + zf_toks)
                    sc_toks.append(sc)
                    y1 = dve.op("tensor_tensor", sml.ap()[:, 42:50], pE.ap()[:, 48:56], ohk, ALU.mult,
                                deps=[b3, sc])
                    y2 = dve.op("reduce_sum", sml.ap()[:, 33:34], sml.ap()[:, 42:50], AX.X, deps=[y1])
                    y3 = dve.op("tensor_tensor", sml.ap()[:, 42:50], pE.ap()[:, 56:64], ohk, ALU.mult,
                                deps=[y2])
                    y4 = dve.op("reduce_sum", sml.ap()[:, 34:35], sml.ap()[:, 42:50], AX.X, deps=[y3])
                    y5 = dve.op("tensor_tensor", sml.ap()[:, 33:34], sml.ap()[:, 33:34], sml.ap()[:, 34:35],
                                 ALU.add, deps=[y2, y4])
                    y6 = dve.op("tensor_tensor", sml.ap()[:, 33:34], sml.ap()[:, 33:34], sml.ap()[:, 36:37],
                                 ALU.add, deps=[y5, s4])
                    yr = dve.op("scalar_tensor_tensor", yrow[:, j:j + 1], eti[:, j:j + 1], float(C),
                                sml.ap()[:, 33:34], ALU.mult, ALU.add, deps=[y6])
                    tok_prev = yr
                agx = em.collective("AllGather", ALU.bypass, GA, [grp_my], [grp_ag], deps=sc_toks + [tok_prev])
                cf32 = sp.dma(sm2.ap()[0:nQ, 864:872], kcnt_ag.rearrange("cc (j e) -> (cc j) e", e=8),
                              deps=[agk, ppd])
                u1 = dve.op("tensor_tensor", sm2.ap()[0:nQ, 872:880], sm2.ap()[0:nQ, 864:872],
                            sm2.ap()[0:nQ, 0:8], ALU.mult, deps=[cf32, lec, tok_prev])
                u2 = dve.op("reduce_sum", sm2.ap()[0:nQ, 880:881], sm2.ap()[0:nQ, 872:880], AX.X, deps=[u1])
                u3 = dve.op("tensor_copy", prb.ap()[0:nQ, 1:2], sm2.ap()[0:nQ, 880:881], deps=[u2])
                u4 = dve.op("tensor_tensor", sm2.ap()[0:nQ, 872:880], sm2.ap()[0:nQ, 864:872],
                            sm2.ap()[0:nQ, 8:16], ALU.mult, deps=[lec2, u3])
                u5 = dve.op("reduce_sum", sm2.ap()[0:nQ, 880:881], sm2.ap()[0:nQ, 872:880], AX.X, deps=[u4])
                u6 = dve.op("tensor_copy", prb.ap()[0:nQ, 2:3], sm2.ap()[0:nQ, 880:881], deps=[u5])
                mcum = pe.op("matmul", pE.ap()[0:8, 24:25], wsa.ap()[0:nQ, 640:648], prb.ap()[0:nQ, 1:2],
                             start=True, stop=True, deps=[ltrg, u3])
                msof = pe.op("matmul", pE.ap()[0:8, 25:26], wsa.ap()[0:nQ, 648:656], prb.ap()[0:nQ, 2:3],
                             start=True, stop=True, deps=[lrsl, u6])
                ev8 = dve.op("tensor_copy", sml.ap()[0:8, 448:450], pE.ap()[0:8, 24:26], deps=[mcum, msof])
                r8d = sp.dma(r8_dram, sml.ap()[0:8, 448:450], deps=[ev8])
                r8r = sp.dma(sml.ap()[0:1, 450:466].rearrange("p (b a) -> p b a", b=2),
                             r8_dram.rearrange("a b -> b a").unsqueeze(0), deps=[r8d],
                             allow_slow_non_contiguous=True)
                xg_toks = [NT]
                for m in range(nC):
                    io1 = pool.op("iota", sm2.ap()[:, 832:833], pattern=[[0, 1]], base=m * 128,
                                  channel_multiplier=1, allow_small_or_imprecise_dtypes=True,
                                  deps=[xg_toks[-1]])
                    bcum = pe.op("matmul", pE.ap()[:, 32:40], onf.ap()[0:1, :], sml.ap()[0:1, 450:458],
                                 start=True, stop=True, deps=[r8r, c_onf, xg_toks[-1], agx])
                    bsof = pe.op("matmul", pE.ap()[:, 40:48], onf.ap()[0:1, :], sml.ap()[0:1, 458:466],
                                 start=True, stop=True)
                    cp1 = dve.op("tensor_scalar", sm2.ap()[:, 833:841], pE.ap()[:, 32:40], sm2.ap()[:, 832:833],
                                 None, ALU.is_le, deps=[bcum, bsof, io1])
                    rsm = dve.op("reduce_sum", sm2.ap()[:, 841:842], sm2.ap()[:, 833:841], AX.X, deps=[cp1])
                    rof = dve.op("tensor_scalar", sm2.ap()[:, 841:842], sm2.ap()[:, 841:842], -1.0, None,
                                 ALU.add, deps=[rsm])
                    ohr = dve.op("tensor_scalar", sm2.ap()[:, 842:850], sml.ap()[:, 0:8], sm2.ap()[:, 841:842],
                                 None, ALU.is_equal, deps=[rof, li8])
                    sc1 = dve.op("tensor_tensor", sm2.ap()[:, 850:858], sm2.ap()[:, 842:850], pE.ap()[:, 32:40],
                                 ALU.mult, deps=[ohr])
                    sc2 = dve.op("reduce_sum", sm2.ap()[:, 858:859], sm2.ap()[:, 850:858], AX.X, deps=[sc1])
                    sc3 = dve.op("tensor_tensor", sm2.ap()[:, 850:858], sm2.ap()[:, 842:850], pE.ap()[:, 40:48],
                                 ALU.mult, deps=[sc2])
                    sc4 = dve.op("reduce_sum", sm2.ap()[:, 859:860], sm2.ap()[:, 850:858], AX.X, deps=[sc3])
                    w1_ = dve.op("scalar_tensor_tensor", sm2.ap()[:, 860:861], sm2.ap()[:, 841:842], float(T),
                                 sm2.ap()[:, 832:833], ALU.mult, ALU.add, deps=[sc4])
                    w2_ = dve.op("tensor_tensor", sm2.ap()[:, 861:862], sm2.ap()[:, 860:861], sm2.ap()[:, 858:859],
                                 ALU.subtract, deps=[w1_])
                    w3_ = dve.op("tensor_tensor", sm2.ap()[:, 862:863], sm2.ap()[:, 861:862], sm2.ap()[:, 859:860],
                                 ALU.add, deps=[w2_])
                    w4_ = dve.op("tensor_scalar", sm2.ap()[:, 862:863], sm2.ap()[:, 862:863], float(8 * T - 1),
                                 0.0, ALU.min, ALU.max, deps=[w3_])
                    ii = dve.op("tensor_copy", idxi.ap()[:, 0:1], sm2.ap()[:, 862:863], deps=[w4_])
                    gx = pool.idma(stg.ap()[:, (m % 2) * D:(m % 2) * D + D], None,
                                   grp_ag, IndirectOffsetOnAxis(ap=idxi.ap()[:, 0:1], axis=0),
                                   deps=[ii, agx] + xg_toks[-2:-1])
                    xg_toks.append(sp.dma(xe_dram[m * 128:(m + 1) * 128, :],
                                          stg.ap()[:, (m % 2) * D:(m % 2) * D + D], deps=[gx]))
                xf_toks = []
                for ch in range(nD):
                    xf_toks.append(act.dmat(tfm.ap()[:, ch * C:(ch + 1) * C],
                                            xe_dram[:, ch * 128:(ch + 1) * 128],
                                            deps=xg_toks + [pe.tok()]))
                yf_dmas = []
                w1par = [NT, NT]
                w2tok = NT
                for sb in range(nSB):
                    hv_toks = []
                    for hc in range(nF):
                        par = hc % 2
                        w1s = wsb.ap()[:, par * 1024:par * 1024 + 1024]
                        lw = pool.dma(w1s.rearrange("p (dc f) -> p dc f", f=128), g["w1e"][:, hc * 128:(hc + 1) * 128].rearrange("(dc p) f -> p dc f", p=128), deps=[w1par[par]])
                        ph_ = (pA if par == 0 else pB).ap()
                        for dc in range(nD):
                            pe.op("matmul", ph_[:, 0:sbT], w1s[:, dc * 128:(dc + 1) * 128],
                                  tfm.ap()[:, dc * C + sb * sbT:dc * C + (sb + 1) * sbT],
                                  start=(dc == 0), stop=(dc == nD - 1),
                                  deps=(xf_toks + [lw, hv_toks[hc - 2] if hc >= 2 else NT]) if dc == 0 else (),
                                  inc=(dc == nD - 1))
                        mm = pe.tok()
                        w1par[par] = mm
                        hv = act.op("activation", r2.ap()[:, hc * sbT:(hc + 1) * sbT], ph_[:, 0:sbT],
                                    AF.Gelu, deps=[mm])
                        hv_toks.append(hv)
                    ev_toks = []
                    for dc in range(nD):
                        w2s = wsb.ap()[:, 2048:2048 + 4096]
                        lw = pool.dma(w2s.rearrange("p (fc d) -> p fc d", d=128), g["w2e"][:, dc * 128:(dc + 1) * 128].rearrange("(fc p) d -> p fc d", p=128), deps=[w2tok])
                        po = (pC if dc % 2 == 0 else pD).ap()
                        for fc in range(nF):
                            pe.op("matmul", po[:, 0:sbT], w2s[:, fc * 128:(fc + 1) * 128],
                                  r2.ap()[:, fc * sbT:(fc + 1) * sbT],
                                  start=(fc == 0), stop=(fc == nF - 1),
                                  deps=([lw] + hv_toks + [ev_toks[dc - 2] if dc >= 2 else NT]) if fc == 0 else (),
                                  inc=(fc == nF - 1))
                        mm = pe.tok()
                        w2tok = mm
                        ob = stg.ap()[:, (dc % 2) * 1024:(dc % 2) * 1024 + sbT]
                        ev = act.op("activation", ob, po[:, 0:sbT], AF.Copy,
                                    deps=[mm, yf_dmas[-2] if len(yf_dmas) >= 2 else NT])
                        ev_toks.append(ev)
                        yf_dmas.append(sp.dma(yfm_dram[dc * 128:(dc + 1) * 128, sb * sbT:(sb + 1) * sbT],
                                              ob, deps=[ev]))
                ymy_toks = []
                for m in range(nC):
                    ss = stg.ap()[:, 2048 + (m % 2) * 1024:2048 + (m % 2) * 1024 + D]
                    lt = act.dmat(ss, yfm_dram[:, m * 128:(m + 1) * 128],
                                  deps=yf_dmas + ([ymy_toks[-2]] if len(ymy_toks) >= 2 else []))
                    ymy_toks.append(sp.dma(y_my[m * 128:(m + 1) * 128, :], ss, deps=[lt]))
                agy = em.collective("AllGather", ALU.bypass, GA, [y_my], [y_ag], deps=ymy_toks)
                gt_prev = NT
                for j in range(nT):
                    ii = dve.op("tensor_copy", idxi.ap()[:, 0:1], yrow[:, j:j + 1], deps=[agy, gt_prev])
                    gt = pool.idma(stg.ap()[:, (j % 2) * D:(j % 2) * D + D], None,
                                   y_ag, IndirectOffsetOnAxis(ap=idxi.ap()[:, 0:1], axis=0),
                                   deps=[ii, agy, x_tok[j]])
                    xs = xb.ap()[:, j * D:(j + 1) * D]
                    t = dve.op("scalar_tensor_tensor", xs, stg.ap()[:, (j % 2) * D:(j % 2) * D + D],
                               gk[:, j:j + 1], xs, ALU.mult, ALU.add, deps=[gt, x_tok[j], ln_last[j]])
                    x_tok[j] = t
                    gt_prev = t

            def emit_lin():
                import os as _os
                _stage = int(_os.environ.get("LIN_STAGE", "99"))
                tf_toks, _ = emit_ln()
                ph = em.phase()
                szb_t = wsb.ap()[:, 4096:6144].bitcast(F32)

                class _SZ:
                    @staticmethod
                    def ap():
                        return szb_t
                szb = _SZ()
                s_first = [True] * 16
                z_first = [True] * nD
                kv_war = NT
                bias_war = [NT, NT]
                s_last = NT
                for half in range(2):
                    nJh = nT // 2
                    lwk = pool.dma(r4.ap()[:, 0:8192].rearrange("p (ec f) -> p ec f", f=1024), g["lwkT"].rearrange("(ec p) f -> p ec f", p=128), deps=[s_last] + (ph if half == 0 else []))
                    lbk = sp.dma(prb.ap()[:, 512:1024], g["lwkTr"][:, 0:512],
                                 deps=(ph if half == 0 else []) + bias_war)
                    lbk2 = sp.dma(prb.ap()[:, 1024:1536], g["lwkTr"][:, 512:1024])
                    phk_toks = [NT, NT]
                    for jj in range(nJh):
                        j = half * nJh + jj
                        for hf in range(2):
                            pk = (pA if hf == 0 else pB).ap()
                            for ec in range(nD):
                                pe.op("matmul", pk[:, 0:512],
                                      tfm.ap()[:, ec * T + j * 128:ec * T + (j + 1) * 128],
                                      r4.ap()[:, ec * 1024 + hf * 512:ec * 1024 + hf * 512 + 512],
                                      start=(ec == 0), stop=(ec == nD - 1),
                                      deps=(tf_toks + [lwk, s_last, phk_toks[hf]]) if ec == 0 else (),
                                      inc=(ec == nD - 1))
                            mm = pe.tok()
                            bs = prb.ap()[:, 512 + hf * 512:1024 + hf * 512]
                            q1 = dve.op("tensor_tensor", sm2.ap()[:, 0:512], pk[:, 0:512], bs, ALU.add,
                                        deps=[mm, lbk, lbk2, phk_toks[hf]])
                            q2 = dve.op("tensor_scalar", sm2.ap()[:, 512:1024], sm2.ap()[:, 0:512], 0.0, None,
                                        ALU.min, deps=[q1])
                            q3 = act.op("activation", prb.ap()[:, 0:512], sm2.ap()[:, 512:1024], AF.Exp,
                                        deps=[q2])
                            q4 = dve.op("tensor_scalar", sm2.ap()[:, 512:1024], sm2.ap()[:, 0:512], 0.0, None,
                                        ALU.max, deps=[q3])
                            q5 = dve.op("scalar_tensor_tensor",
                                        r2.ap()[:, jj * 1024 + hf * 512:jj * 1024 + hf * 512 + 512],
                                        sm2.ap()[:, 512:1024], 1.0, prb.ap()[:, 0:512], ALU.mult, ALU.add,
                                        deps=[q4, kv_war])
                            phk_toks[hf] = q5
                    kphase = pe.tok()
                    lwv = pool.dma(r4.ap()[:, 0:8192].rearrange("p (ec f) -> p ec f", f=1024), g["lwvT"].rearrange("(ec p) f -> p ec f", p=128), deps=[kphase, phk_toks[0], phk_toks[1]])
                    lbv = sp.dma(prb.ap()[:, 512:1024], g["lwvTr"][:, 0:512],
                                 deps=[kphase] + phk_toks)
                    lbv2 = sp.dma(prb.ap()[:, 1024:1536], g["lwvTr"][:, 512:1024],
                                  deps=[kphase])
                    v_toks = [NT, NT]
                    for jj in range(nJh):
                        j = half * nJh + jj
                        for hf in range(2):
                            pk = (pA if hf == 0 else pB).ap()
                            for ec in range(nD):
                                pe.op("matmul", pk[:, 0:512],
                                      tfm.ap()[:, ec * T + j * 128:ec * T + (j + 1) * 128],
                                      r4.ap()[:, ec * 1024 + hf * 512:ec * 1024 + hf * 512 + 512],
                                      start=(ec == 0), stop=(ec == nD - 1),
                                      deps=(tf_toks + [lwv, v_toks[hf]]) if ec == 0 else (),
                                      inc=(ec == nD - 1))
                            mm = pe.tok()
                            bs = prb.ap()[:, 512 + hf * 512:1024 + hf * 512]
                            vv = dve.op("tensor_tensor",
                                        r3.ap()[:, jj * 1024 + hf * 512:jj * 1024 + hf * 512 + 512],
                                        pk[:, 0:512], bs, ALU.add, deps=[mm, lbv, lbv2, v_toks[hf], kv_war])
                            v_toks[hf] = vv
                    _nosz = _os.environ.get("LIN_NO_SZ", "0") == "1"
                    if half == 0:
                        zinit = dve.op("memset", szb.ap()[:, :], 0.0, deps=ph)
                        zinit2 = dve.op("memset", sml.ap()[:, 100:108], 0.0, deps=ph)
                        sadd = zinit2
                    sz_deps = [phk_toks[0], phk_toks[1], v_toks[0], v_toks[1]]
                    for jj in range(nJh if not _nosz else 0):
                        for h in range(16):
                            ps = (pC if h < 8 else pD).ap()
                            hh = h % 8
                            pe.op("matmul", ps[0:64, hh * 64:(hh + 1) * 64],
                                  r2.ap()[:, jj * 1024 + h * 64:jj * 1024 + (h + 1) * 64],
                                  r3.ap()[:, jj * 1024 + h * 64:jj * 1024 + (h + 1) * 64],
                                  start=True, stop=True,
                                  deps=(sz_deps + [sadd]) if h == 0 else (), inc=(h == 15))
                        mmS = pe.tok()
                        for fc in range(nD):
                            pe.op("matmul", pF.ap()[:, fc:fc + 1],
                                  r2.ap()[:, jj * 1024 + fc * 128:jj * 1024 + (fc + 1) * 128],
                                  onc.ap()[:, 0:1],
                                  start=True, stop=True,
                                  deps=[c_onc] if fc == 0 else (), inc=(fc == nD - 1))
                        mmZ = pe.tok()
                        a1_ = dve.op("tensor_tensor", szb.ap()[0:64, 0:512], szb.ap()[0:64, 0:512],
                                     pC.ap()[0:64, :], ALU.add, deps=[mmS, sadd])
                        a2_ = dve.op("tensor_tensor", szb.ap()[0:64, 512:1024], szb.ap()[0:64, 512:1024],
                                     pD.ap()[0:64, :], ALU.add)
                        a3_ = dve.op("tensor_tensor", sml.ap()[:, 100:108], sml.ap()[:, 100:108],
                                     pF.ap()[:, 0:8], ALU.add, deps=[mmZ])
                        sadd = a3_
                    s_last = pe.tok()
                    kv_war = s_last
                    bias_war = [v_toks[0], v_toks[1], sadd]
                if _stage <= 1:
                    return
                szd1 = sp.dma(sz_my[0:64, 0:512], szb.ap()[0:64, 0:512], deps=[sadd])
                szd2 = sp.dma(sz_my[64:128, 0:512], szb.ap()[0:64, 512:1024], deps=[sadd])
                szd3 = sp.dma(sz_my[:, 512:520], sml.ap()[:, 100:108], deps=[sadd])
                agr = em.collective("AllReduce", ALU.add, GB, [sz_my], [sz_red], deps=[szd1, szd2, szd3])
                if _stage <= 2:
                    return
                saug_toks = []
                zprb = dve.op("memset", prb.ap()[:, 0:2048], 0.0, deps=[s_last, sadd])
                prb_v = prb.ap()[:, 0:16 * 72].rearrange("p (hf two c) -> p hf two c", two=2, c=72)
                for hq in range(2):
                    for hp in range(2):
                        srcS = sz_red[hq * 64:(hq + 1) * 64, 0:512] \
                            .rearrange("p (cf two c) -> p cf two c", two=2, c=64)[:, :, hp, :]
                        dstS = prb_v[hp * 64:hp * 64 + 64, 4 * hq:4 * hq + 4, hp, 0:64]
                        saug_toks.append(pool.dma(dstS, srcS, deps=[agr, zprb]))
                for hp in range(2):
                    srcZ = sz_red[hp * 64:hp * 64 + 64, 512:520]
                    dstZ = prb_v[hp * 64:hp * 64 + 64, :, hp, 64:65]
                    saug_toks.append(pool.dma(dstZ, srcZ, deps=[agr, zprb], allow_slow_non_contiguous=True))
                if _stage <= 3:
                    return
                lwq = pool.dma(r4.ap()[:, 0:8192].rearrange("p (ec f) -> p ec f", f=1024),
                               g["lwqT"].rearrange("(ec p) f -> p ec f", p=128),
                               deps=[s_last] + bias_war)
                lbq = sp.dma(stg.ap()[:, 3072:3584], g["lwqTr"][:, 0:512], deps=bias_war)
                lbq2 = sp.dma(stg.ap()[:, 3584:4096], g["lwqTr"][:, 512:1024])
                pq_toks = [NT, NT]
                pqd = []
                for j in range(nT):
                    for hf in range(2):
                        pk = (pA if hf == 0 else pB).ap()
                        for ec in range(nD):
                            pe.op("matmul", pk[:, 0:512],
                                  tfm.ap()[:, ec * T + j * 128:ec * T + (j + 1) * 128],
                                  r4.ap()[:, ec * 1024 + hf * 512:ec * 1024 + hf * 512 + 512],
                                  start=(ec == 0), stop=(ec == nD - 1),
                                  deps=(tf_toks + [lwq, s_last, pq_toks[hf]] + bias_war) if ec == 0 else (),
                                  inc=(ec == nD - 1))
                        mm = pe.tok()
                        bs = stg.ap()[:, 3072 + hf * 512:3584 + hf * 512]
                        q1 = dve.op("tensor_tensor", sm2.ap()[:, 0:512], pk[:, 0:512], bs, ALU.add,
                                    deps=[mm, lbq, lbq2, pq_toks[hf]])
                        q2 = dve.op("tensor_scalar", sm2.ap()[:, 512:1024], sm2.ap()[:, 0:512], 0.0, None,
                                    ALU.min, deps=[q1])
                        q3 = act.op("activation", prb.ap()[:, 1536:2048], sm2.ap()[:, 512:1024], AF.Exp,
                                    deps=[q2])
                        q4 = dve.op("tensor_scalar", sm2.ap()[:, 512:1024], sm2.ap()[:, 0:512], 0.0, None,
                                    ALU.max, deps=[q3])
                        q5 = dve.op("scalar_tensor_tensor",
                                    r2.ap()[:, j * 1024 + hf * 512:j * 1024 + hf * 512 + 512],
                                    sm2.ap()[:, 512:1024], 1.0, prb.ap()[:, 1536:2048], ALU.mult, ALU.add,
                                    deps=[q4, sadd])
                        pq_toks[hf] = q5
                    pqd.append(sp.dma(t_dram[j * 128:(j + 1) * 128, :],
                                      r2.ap()[:, j * 1024:(j + 1) * 1024],
                                      deps=[pq_toks[0], pq_toks[1]]))
                qf2 = []
                for ch in range(nD):
                    qf2.append(act.dmat(tfm.ap()[:, ch * T:(ch + 1) * T],
                                        t_dram[:, ch * 128:(ch + 1) * 128],
                                        deps=pqd + [pe.tok()]))
                lqc = sp.dma(sml.ap()[:, 16:24], g["lwqTc"], deps=[])
                ctx_dmas = []
                nm_war = NT
                _ln = int(_os.environ.get("LIN_NUM", "9"))
                for fcq in range(min(nD, int(_os.environ.get("LIN_FCQ", "99"))) if _ln >= 1 else 0):
                    for j in range(min(nT, int(_os.environ.get('LIN_NJ', '99')))):
                        pn0 = (pE if j % 2 == 0 else pC).ap()
                        pn1 = (pF if j % 2 == 0 else pD).ap()
                        for hh in range(min(2, int(_os.environ.get('LIN_NH', '2')))):
                            h = 2 * fcq + hh
                            _rhs = (prb.ap()[hh * 64:hh * 64 + 64, h * 72:h * 72 + 66]
                                    if _os.environ.get("LNRHS", "prb") == "prb" else
                                    tfm.ap()[hh * 64:hh * 64 + 64, 0:66])
                            pe.op("matmul", (pn0 if hh == 0 else pn1)[:, 0:66],
                                  tfm.ap()[hh * 64:hh * 64 + 64, fcq * T + j * 128:fcq * T + (j + 1) * 128],
                                  _rhs,
                                  start=True, stop=True,
                                  deps=(qf2 + saug_toks + ctx_dmas[-2:-1]) if hh == 0 else (),
                                  inc=False)
                        mmn = pe.op("nop")
                        if _ln <= 2:
                            ctx_dmas.append(mmn)
                            continue
                        d1a = dve.op("tensor_scalar", sml.ap()[:, 24:25], pn0[:, 64:65], 1e-6, None,
                                     ALU.add, deps=[mmn])
                        d1 = dve.op("tensor_scalar", sml.ap()[:, 25:26], pn1[:, 64:65], 1e-6, None,
                                    ALU.add, deps=[d1a])
                        if _ln <= 3:
                            ctx_dmas.append(d1)
                            continue
                        d2 = dve.op("reciprocal", sml.ap()[:, 26:28], sml.ap()[:, 24:26], deps=[d1])
                        if _ln <= 4:
                            ctx_dmas.append(d2)
                            continue
                        slo = (j % 2) * 128
                        o1 = dve.op("tensor_scalar", stg.ap()[:, slo:slo + 64], pn0[:, 0:64],
                                    sml.ap()[:, 26:27], None, ALU.mult, deps=[d2] + ctx_dmas[-2:-1])
                        o2 = dve.op("tensor_scalar", stg.ap()[:, slo + 64:slo + 128], pn1[:, 0:64],
                                    sml.ap()[:, 27:28], None, ALU.mult, deps=[o1])
                        if _ln <= 5:
                            ctx_dmas.append(o2)
                            continue
                        ctx_dmas.append(sp.dma(lctx_dram[j * 128:(j + 1) * 128, fcq * 128:(fcq + 1) * 128],
                                               stg.ap()[:, slo:slo + 128], deps=[o2]))
                if _stage <= 4:
                    return
                lboc = sp.dma(sml.ap()[:, 28:36], g["lwoTc"], deps=[])
                ctxs = r2.ap()[:, 0:4096]
                oev_prev = [NT, NT]
                odma = []
                for tb in range(4):
                    cld = []
                    for ec in range(nD):
                        cld.append(act.dmat(ctxs[:, ec * 512:ec * 512 + tbT],
                                            lctx_dram[tb * tbT:(tb + 1) * tbT, ec * 128:(ec + 1) * 128],
                                            deps=ctx_dmas + [s_last] + (odma[-8:] if tb else [])))
                    for oc in range(nD):
                        par = oc % 2
                        wst = wsb.ap()[:, 2048 + par * 1024:2048 + par * 1024 + 1024]
                        lw = pool.dma(wst.rearrange("p (ec o) -> p ec o", o=128), g["lwoT"][:, oc * 128:(oc + 1) * 128].rearrange("(ec p) o -> p ec o", p=128), deps=[oev_prev[par]])
                        po = (pC if par == 0 else pD).ap()
                        for ec in range(nD):
                            pe.op("matmul", po[:, 0:tbT], wst[:, ec * 128:(ec + 1) * 128],
                                  ctxs[:, ec * 512:ec * 512 + tbT],
                                  start=(ec == 0), stop=(ec == nD - 1),
                                  deps=(cld + [lw, oev_prev[par]]) if ec == 0 else (), inc=(ec == nD - 1))
                        mm = pe.tok()
                        ob = stg.ap()[:, 2048 + par * 512:2048 + par * 512 + tbT]
                        ev = dve.op("tensor_scalar", ob, po[:, 0:tbT], sml.ap()[:, 28 + oc:29 + oc], None,
                                    ALU.add, deps=[mm, lboc, odma[-2] if len(odma) >= 2 else NT])
                        oev_prev[par] = ev
                        odma.append(sp.dma(o_dram[oc * 128:(oc + 1) * 128, tb * tbT:(tb + 1) * tbT], ob,
                                           deps=[ev]))
                emit_resid(odma)

            for p in parts:
                if p in ("sw1", "sw2"):
                    emit_swiglu()
                elif p == "attn":
                    emit_attn()
                elif p in ("moe1", "moe2"):
                    emit_moe()
                elif p == "lin":
                    emit_lin()

            for j in range(nT):
                sp.dma(xout[j * 128:(j + 1) * 128, :], xb.ap()[:, j * D:(j + 1) * D], deps=[x_tok[j]])
            for e in (sp, act, pool):
                for s, n in e.dma_all_tok():
                    sp.op("wait_ge", s, n, inc=False)
            sp._wait_all([(em.cc_sem, em.cc_n)])
    return nc


PART_INPUTS = {
    "sw1": ["wgT", "wuT", "wdT"],
    "sw2": ["wgT", "wuT", "wdT"],
    "attn": ["wqa", "wqbp", "wkva", "wkbp", "wvb", "wout", "bout",
             "qcos", "qsin", "kcos", "ksin", "cmask"],
    "moe1": ["router", "w1e", "w2e", "triT", "mslot", "mlocal", "rbcol", "idx8", "amat",
             "emcol", "emcol2", "trig", "rsel"],
    "moe2": ["router", "w1e", "w2e", "triT", "mslot", "mlocal", "rbcol", "idx8", "amat",
             "emcol", "emcol2", "trig", "rsel"],
    "lin": [w for nm in ("lwqT", "lwkT", "lwvT", "lwoT") for w in (nm, nm + "c", nm + "r")],
}

FUSED = False


def run_groups(cfg, groups, inputs):
    sh, pc = host_prep(inputs, cfg)
    xs = [pc[core]["xin"] for core in range(8)]
    for parts in groups:
        nc = build_program(cfg, parts)
        names = sorted({n for p in parts for n in PART_INPUTS[p]})
        in_maps = []
        for core in range(8):
            m = {"xin": xs[core]}
            for n in names:
                m[n] = pc[core][n] if n in pc[core] else sh[n]
            in_maps.append(m)
        res = run_bass_kernel_spmd(nc, in_maps, core_ids=list(range(8)))
        xs = [np.asarray(r["xout"]) for r in res.results]
    return xs


def kernel(**inputs):
    cfg = make_cfg(8192)
    groups = [list(ALL_PARTS)] if FUSED else [[p] for p in ALL_PARTS]
    xs = run_groups(cfg, groups, inputs)
    B, Sb, D, nT = cfg["B"], cfg["Sb"], cfg["D"], cfg["nT"]
    out = np.zeros((B, Sb, D), np.float32)
    for core in range(8):
        b, r = core // 4, core % 4
        o = xs[core].reshape(nT, 128, D)
        for j in range(nT):
            s0 = (4 * j + r) * 128
            out[b, s0:s0 + 128, :] = o[j]
    return out
